# version 10 (final)
# speedup vs baseline: 1.6078x; 1.0217x over previous
import numpy as np
from contextlib import ExitStack

import concourse.bass as bass
import concourse.mybir as mybir
from concourse.alu_op_type import AluOpType as OP
from concourse.bass_utils import run_bass_kernel_spmd

F32 = mybir.dt.float32
BF16 = mybir.dt.bfloat16
AF = mybir.ActivationFunctionType
AX = mybir.AxisListType.X

P, D, DC, NSEQ, T, I, IC = 128, 1024, 8, 4, 2048, 2816, 22
E10 = float(np.exp(np.float32(10.0)))
LN5 = float(np.log(np.float32(5.0)))
EPS_WKV, LN_EPS = 1e-06, 1e-05
TT2, TT3, TTB = 128, 64, 512
NT2, NT3, NTB = T // TT2, T // TT3, NSEQ * T // TTB
TBS = T // TTB
COLS2 = NSEQ * TT2
HC2 = COLS2 // 2
INV_D = float(np.float32(1.0) / np.float32(D))
N_ENG = 5
DEBUG_XP = False
A2_LIMIT = None
A3_ON = True
A2_CUT = "full"


def build(phase_limit=99):
    A2L = min(NT2, A2_LIMIT) if A2_LIMIT else NT2
    nc = bass.Bass("TRN2", target_bir_lowering=False)

    xTok = nc.dram_tensor("xTok", [NSEQ, T, D], F32, kind="ExternalInput").ap()
    xT = nc.dram_tensor("xT", [NSEQ, D, T], F32, kind="ExternalInput").ap()
    wIn = [nc.dram_tensor(nm, [D, D], F32, kind="ExternalInput").ap()
           for nm in ("wR", "wK", "wV", "wTd")]
    gdec = nc.dram_tensor("gdec", [P, DC], F32, kind="ExternalInput").ap()
    mixc = nc.dram_tensor("mixc", [P, DC], F32, kind="ExternalInput").ap()
    wg = nc.dram_tensor("wg", [I, D], F32, kind="ExternalInput").ap()
    wu = nc.dram_tensor("wu", [I, D], F32, kind="ExternalInput").ap()
    wd = nc.dram_tensor("wd", [D, I], F32, kind="ExternalInput").ap()
    cc = nc.dram_tensor("cc", [NSEQ, D], F32, kind="ExternalInput").ap()
    wmod = nc.dram_tensor("wmod", [D, 6 * D], F32, kind="ExternalInput").ap()
    out = nc.dram_tensor("out", [P, DC, NSEQ, T], F32,
                         kind="ExternalOutput").ap()

    q_hbm = {nm: nc.dram_tensor("q_" + nm, [P, DC, NSEQ, T], F32,
                                kind="ExternalOutput" if DEBUG_XP
                                else "Internal").ap()
             for nm in ("ek", "ekv", "td", "r")}
    xp_hbm = nc.dram_tensor("xp", [P, DC, NSEQ, T], F32,
                            kind="ExternalOutput" if DEBUG_XP
                            else "Internal").ap()
    wx_hbm = nc.dram_tensor("wxq", [4, P, DC, D], BF16, kind="Internal").ap()
    wgu_hbm = nc.dram_tensor("wguq", [P, DC, 5632], BF16, kind="Internal").ap()
    wd_hbm = nc.dram_tensor("wdq", [P, IC, D], BF16, kind="Internal").ap()

    st_ = ExitStack()
    with st_:
        sb = lambda n_, s, d=F32: st_.enter_context(nc.sbuf_tensor(n_, s, d))
        ps_ = lambda n_, s, d=F32: st_.enter_context(nc.psum_tensor(n_, s, d))
        SM = {}
        for s in (["ld", "gp", "p0v", "p0s", "p0t", "p0d", "p0o", "p0e", "ph",
                  "a2x", "a2v", "a2s", "a2t", "a2w", "a2d",
                  "a3q", "a3v", "a3s", "a3d", "a3x",
                  "wqd", "wqv", "wqs", "wqt", "wqo",
                  "bx", "bw", "bwd", "bv", "bs", "bt", "bo"]
                  + ["pm%d" % i for i in range(2)]
                  + ["pw%d" % i for i in range(2)]
                  + ["po%d" % i for i in range(2)]
                  + ["ax%d" % i for i in range(8)]
                  + ["aw%d" % i for i in range(2)]
                  + ["adq%d" % i for i in range(8)]
                  + ["wo%d" % i for i in range(2)]
                  + ["bw0", "bw1", "bd0", "bd1"]):
            SM[s] = st_.enter_context(nc.semaphore("s_" + s))

        identb = sb("identb", [P, P], BF16)
        identf = sb("identf", [P, P], F32)
        zcol = sb("zcol", [P, 1], F32)
        c_lneps = sb("c_lneps", [P, 1], F32)
        modkeep = sb("modkeep", [NSEQ, 2 * D], F32)
        modT = sb("modT", [P, DC, 6, NSEQ], F32)
        gdcol = sb("gdcol", [P, DC], F32)
        ssel = sb("ssel", [NSEQ, NSEQ, P], F32)
        c2col = sb("c2col", [P, DC], F32)
        bc1 = sb("bc1", [P, D], F32)
        trf_st = ExitStack()
        ps_trf = trf_st.enter_context(nc.psum_tensor("ps_trf",
                                                     [P, 2, 512], F32))

        PL = {e: [] for e in ("sync", "vector", "scalar", "tensor", "gpsimd")}

        def em(eng):
            def deco(fn):
                PL[eng].append(fn)
            return deco

        PHASE = [0]

        def barrier():
            PHASE[0] += 1
            lvl = PHASE[0] * N_ENG

            def mk(eng):
                @em(eng)
                def _(e):
                    e.drain().then_inc(SM["ph"], 1)
                @em(eng)
                def _(e):
                    e.wait_ge(SM["ph"], lvl)
            for eng in PL:
                mk(eng)

        p0_st = ExitStack()
        p0w = [p0_st.enter_context(nc.sbuf_tensor("p0w%d" % i, [P, D], F32))
               for i in range(2)]
        p0wb = [p0_st.enter_context(nc.sbuf_tensor("p0wb%d" % i, [P, D], BF16))
                for i in range(2)]
        p0m = [p0_st.enter_context(nc.sbuf_tensor("p0m%d" % i, [P, 3 * D],
                                                  F32)) for i in range(2)]
        p0row = p0_st.enter_context(nc.sbuf_tensor("p0row", [NSEQ, 6 * D],
                                                   F32))
        p0sc = p0_st.enter_context(nc.sbuf_tensor("p0sc", [NSEQ, 2 * D], F32))
        cTc = p0_st.enter_context(nc.sbuf_tensor("cTc", [P, DC, NSEQ], F32))
        gdraw = p0_st.enter_context(nc.sbuf_tensor("gdraw", [P, DC], F32))
        ps_mod = p0_st.enter_context(nc.psum_tensor("ps_mod", [NSEQ, 3 * D],
                                                    F32))

        @em("gpsimd")
        def _(g):
            g.memset(identb[:], 0.0)
            g.drain()
            g.affine_select(out=identb[:], in_=identb[:],
                            compare_op=OP.not_equal, fill=1.0, base=0,
                            pattern=[[-1, P]], channel_multiplier=1)
            g.drain()
            g.memset(identf[:], 0.0)
            g.drain()
            g.affine_select(out=identf[:], in_=identf[:],
                            compare_op=OP.not_equal, fill=1.0, base=0,
                            pattern=[[-1, P]], channel_multiplier=1)
            g.drain()
            g.memset(zcol[:], 0.0)
            g.memset(c_lneps[:], LN_EPS)
            g.memset(ssel[:], 0.0)
            g.drain()
            g.affine_select(out=ssel[:], in_=ssel[:],
                            compare_op=OP.not_equal, fill=1.0, base=0,
                            pattern=[[-1, NSEQ], [0, P]],
                            channel_multiplier=1)
            g.drain().then_inc(SM["gp"], 1)

        @em("sync")
        def _(s):
            s.dma_start(out=p0w[0][0:NSEQ, :], in_=cc[:, :]).then_inc(
                SM["ld"], 16)
            s.dma_start(out=gdraw[:], in_=gdec[:, :]).then_inc(SM["ld"], 16)
            s.dma_start(out=c2col[:], in_=mixc[:, :]).then_inc(SM["ld"], 16)

        @em("scalar")
        def _(s):
            s.wait_ge(SM["ld"], 48)
            s.activation(out=p0w[1][0:NSEQ, :], in_=p0w[0][0:NSEQ, :],
                         func=AF.Silu)
            s.activation(out=gdcol[:], in_=gdraw[:], func=AF.Exp)
            s.drain().then_inc(SM["p0s"], 1)

        @em("vector")
        def _(v):
            v.wait_ge(SM["ld"], 48)
            v.tensor_scalar(out=c2col[:], in0=c2col[:], scalar1=1.0,
                            scalar2=None, op0=OP.subtract)
            v.drain().then_inc(SM["p0v"], 1)

        @em("tensor")
        def _(t):
            t.wait_ge(SM["gp"], 1)
            t.wait_ge(SM["p0s"], 1)
            for dsub in range(DC):
                if dsub >= 2:
                    t.wait_ge(SM["p0e"], dsub - 1)
                t.transpose(ps_trf[:, dsub % 2, 0:NSEQ],
                            p0w[1][0:NSEQ, dsub * P:(dsub + 1) * P],
                            identf[0:NSEQ, 0:NSEQ]).then_inc(SM["p0t"], 1)

        @em("vector")
        def _(v):
            for dsub in range(DC):
                v.wait_ge(SM["p0t"], dsub + 1)
                v.tensor_copy(out=cTc[:, dsub, :],
                              in_=ps_trf[:, dsub % 2, 0:NSEQ])
                v.drain().then_inc(SM["p0e"], 1)
            v.drain().then_inc(SM["p0v"], 1)

        @em("sync")
        def _(s):
            for half in range(2):
                for dsub in range(DC):
                    idx = half * DC + dsub
                    if idx >= 2:
                        s.wait_ge(SM["p0t"], DC + idx - 1)
                    s.dma_start(out=p0m[idx % 2][:, :],
                                in_=wmod[dsub * P:(dsub + 1) * P,
                                         half * 3072:(half + 1) * 3072]
                                ).then_inc(SM["pm%d" % (idx % 2)], 16)

        @em("tensor")
        def _(t):
            t.wait_ge(SM["p0v"], 2)
            for half in range(2):
                if half == 1:
                    t.wait_ge(SM["p0v"], 3)
                for dsub in range(DC):
                    idx = half * DC + dsub
                    t.wait_ge(SM["pm%d" % (idx % 2)], 16 * (idx // 2 + 1))
                    mi = None
                    for piece in range(6):
                        mi = t.matmul(
                            ps_mod[:, piece * 512:(piece + 1) * 512],
                            cTc[:, dsub, :],
                            p0m[idx % 2][:, piece * 512:(piece + 1) * 512],
                            start=(dsub == 0), stop=(dsub == DC - 1))
                    mi.then_inc(SM["p0t"], 1)

        @em("vector")
        def _(v):
            for half in range(2):
                v.wait_ge(SM["p0t"], DC + (half + 1) * DC)
                v.tensor_copy(out=p0row[:, half * 3072:(half + 1) * 3072],
                              in_=ps_mod[:, :])
                v.drain().then_inc(SM["p0v"], 1)
            v.tensor_scalar(out=p0row[:, D:2 * D], in0=p0row[:, D:2 * D],
                            scalar1=1.0, scalar2=None, op0=OP.add)
            v.tensor_scalar(out=p0row[:, 4 * D:5 * D],
                            in0=p0row[:, 4 * D:5 * D],
                            scalar1=1.0, scalar2=None, op0=OP.add)
            v.drain().then_inc(SM["p0v"], 1)

        @em("scalar")
        def _(s):
            s.wait_ge(SM["p0v"], 5)
            for j, slot in enumerate((2, 5)):
                s.activation(out=p0sc[:, j * D:(j + 1) * D],
                             in_=p0row[:, slot * D:(slot + 1) * D],
                             func=AF.Sign)
            s.drain().then_inc(SM["p0s"], 1)

        @em("vector")
        def _(v):
            v.wait_ge(SM["p0s"], 2)
            for j, slot in enumerate((2, 5)):
                sl = p0row[:, slot * D:(slot + 1) * D]
                v.scalar_tensor_tensor(out=sl, in0=sl, scalar=-1.0, in1=sl,
                                       op0=OP.mult, op1=OP.max)
                v.drain()
                zb4 = zcol[0:NSEQ, :].broadcast_to([NSEQ, D])
                v.scalar_tensor_tensor(out=sl, in0=sl, scalar=0.25, in1=zb4,
                                       op0=OP.max, op1=OP.add)
                v.drain()
                v.tensor_tensor(out=sl, in0=sl,
                                in1=p0sc[:, j * D:(j + 1) * D], op=OP.mult)
                v.drain()
            v.tensor_copy(out=modkeep[:, 0:D], in_=p0row[:, 3 * D:4 * D])
            v.tensor_copy(out=modkeep[:, D:2 * D], in_=p0row[:, 4 * D:5 * D])
            v.drain().then_inc(SM["p0v"], 1)

        @em("tensor")
        def _(t):
            t.wait_ge(SM["p0v"], 6)
            n = 0
            for slot in range(6):
                for dsub in range(DC):
                    if n >= 2:
                        t.wait_ge(SM["p0e"], DC + n - 1)
                    t.transpose(ps_trf[:, n % 2, 0:NSEQ],
                                p0row[0:NSEQ, slot * D + dsub * P:
                                      slot * D + (dsub + 1) * P],
                                identf[0:NSEQ, 0:NSEQ]).then_inc(SM["p0t"], 1)
                    n += 1

        @em("vector")
        def _(v):
            n = 0
            for slot in range(6):
                for dsub in range(DC):
                    v.wait_ge(SM["p0t"], 3 * DC + n + 1)
                    v.tensor_copy(out=modT[:, dsub, slot, :],
                                  in_=ps_trf[:, n % 2, 0:NSEQ])
                    v.drain().then_inc(SM["p0e"], 1)
                    n += 1
            v.drain().then_inc(SM["p0v"], 1)

        @em("sync")
        def _(s):
            for k in range(32):
                if k >= 2:
                    s.wait_ge(SM["p0s"], 2 + k - 1)
                qi, dsub = divmod(k, DC)
                s.dma_start(out=p0w[k % 2][:, :],
                            in_=wIn[qi][dsub * P:(dsub + 1) * P, :]
                            ).then_inc(SM["pw%d" % (k % 2)], 16)

        @em("scalar")
        def _(s):
            for k in range(32):
                s.wait_ge(SM["pw%d" % (k % 2)], 16 * (k // 2 + 1))
                if k >= 2:
                    s.wait_ge(SM["po%d" % (k % 2)], 16 * (k // 2))
                s.copy(out=p0wb[k % 2][:, :], in_=p0w[k % 2][:, :])
                s.drain().then_inc(SM["p0s"], 1)
                qi, dsub = divmod(k, DC)
                s.dma_start(out=wx_hbm[qi, :, dsub, :],
                            in_=p0wb[k % 2][:, :]).then_inc(
                    SM["po%d" % (k % 2)], 16)

        @em("sync")
        def _(s):
            for k2 in range(2):
                s.wait_ge(SM["po%d" % k2], 16 * 16)

        p0_st.close()
        barrier()
        PHB = {1: {e: len(PL[e]) for e in PL}}

        a_st = ExitStack()
        asb = lambda n_, s, d=F32: a_st.enter_context(
            nc.sbuf_tensor(n_, s, d))
        a2_xt = [asb("a2_xt%d" % i, [P, D]) for i in range(8)]
        a2_hF = asb("a2_hF", [P, DC, NSEQ, TT2])
        a2_dt = asb("a2_dt", [P, NSEQ, TT2])
        a2_xm = asb("a2_xm", [P, DC, NSEQ, TT2], BF16)
        a2_hc = asb("a2_hc", [P, DC, NSEQ, 1])
        a2_w = [asb("a2_w%d" % i, [P, DC, P], BF16) for i in range(8)]
        a2_stg = asb("a2_stg", [P, 2, 4, HC2], F32)
        a2_ek = asb("a2_ek", [P, 2, HC2], F32)
        a2_c = asb("a2_c", [P, 20], F32)
        a3_q = {nm: asb("a3q_" + nm, [P, 2, DC, NSEQ, TT3])
                for nm in ("ek", "ekv", "td", "r")}
        a3_a = asb("a3_a", [P, DC, NSEQ, TT3])
        a3_b = asb("a3_b", [P, DC, NSEQ, TT3])
        a3_x = asb("a3_x", [P, DC, NSEQ, TT3])
        a3_mh = asb("a3_mh", [P, DC, NSEQ, TT3])
        a3_cr = asb("a3_cr", [P, DC, NSEQ, 4])
        ps_qkv = a_st.enter_context(
            nc.psum_tensor("ps_qkv", [P, 3, 4, HC2], F32))

        A2V = 3 + 16
        A2S = 1 + 32 + 16
        A2T = 32 + 16

        def a2v_z(it): return it * A2V + 1
        def a2v_t1(it): return it * A2V + 2
        def a2v_xm(it): return it * A2V + 3
        def a2v_ep(it, idx): return it * A2V + 3 + idx + 1
        def a2s_invs(it): return it * A2S + 1
        def a2s_hF(it, j=31): return it * A2S + 1 + j + 1
        def a2s_ep(it, idx): return it * A2S + 33 + idx + 1
        def a2t_tr(it, j): return it * A2T + j + 1
        def a2t_mm(it, idx): return it * A2T + 32 + idx + 1
        def a2d_n(it, idx, j=3):
            return 16 * (it * 64 + idx * 4 + j + 1)

        @em("gpsimd")
        def _(g):
            g.memset(a2_hc[:], 0.0)
            g.memset(a3_cr[:], 0.0)
            g.drain().then_inc(SM["gp"], 1)

        @em("sync")
        def _(s):
            names = ("ek", "ekv", "td", "r")
            n_st3 = [0]
            for it in range(A2L):
                for sq in range(NSEQ):
                    k = it * NSEQ + sq
                    if k >= 8:
                        it2p, sq2p = divmod(k - 8, NSEQ)
                        s.wait_ge(SM["a2t"], it2p * A2T + sq2p * DC + DC)
                    s.dma_start(out=a2_xt[k % 8][:, :],
                                in_=xTok[sq, it * TT2:(it + 1) * TT2, :]
                                ).then_inc(SM["ax%d" % (k % 8)], 16)
                def _wset(ec):
                    st2 = it * DC + ec
                    if st2 >= 2:
                        itp, ecp = divmod(st2 - 2, DC)
                        s.wait_ge(SM["a2t"], a2t_mm(itp, 2 * ecp + 1))
                    for qi in range(4):
                        s.dma_start(out=a2_w[(st2 % 2) * 4 + qi][:, :, :],
                                    in_=wx_hbm[qi, :, :, ec * P:(ec + 1) * P]
                                    ).then_inc(SM["aw%d" % (st2 % 2)], 16)

                def _store(idx):
                    if A2_CUT == "nostore":
                        return
                    ec, hf = divmod(idx, 2)
                    for j in range(4):
                        if j < 2:
                            s.wait_ge(SM["a2v"], a2v_ep(it, idx))
                        else:
                            s.wait_ge(SM["a2s"], a2s_ep(it, idx))
                        s.dma_start(
                            out=q_hbm[names[j]][:, ec, 2 * hf:2 * hf + 2,
                                                it * TT2:(it + 1) * TT2],
                            in_=a2_stg[:, hf, j, :].rearrange(
                                "p (s t) -> p s t", s=2)
                        ).then_inc(SM["adq%d" % (hf * 4 + j)], 16)

                if A2_CUT in ("full", "nostore"):
                    _wset(0)
                    _wset(1)
                    for ec in range(2, DC):
                        _store(2 * (ec - 2))
                        _store(2 * (ec - 2) + 1)
                        _wset(ec)
                    for idx in range(12, 16):
                        _store(idx)

        @em("gpsimd")
        def _(g):
            if not A3_ON:
                return
            names = ("ek", "ekv", "td", "r")

            def _a3_load(it3):
                if it3 >= 2:
                    g.wait_ge(SM["a3v"], a3v(it3 - 2, 4))
                for sl8 in range(8):
                    g.wait_ge(SM["adq%d" % sl8],
                              16 * 8 * (it3 // 2 + 1))
                for nm in names:
                    g.dma_start(out=a3_q[nm][:, it3 % 2],
                                in_=q_hbm[nm][:, :, :,
                                              it3 * TT3:(it3 + 1) * TT3]
                                ).then_inc(SM["a3q"], 16)
                if it3 >= 1:
                    g.wait_ge(SM["a3v"], a3v(it3 - 1, 4))
                    g.wait_ge(SM["a3d"], 16 * it3)
                for sq2 in range(NSEQ):
                    g.dma_start(out=a3_x[:, :, sq2, :],
                                in_=xT[sq2, :, it3 * TT3:(it3 + 1) * TT3]
                                .rearrange("(c p) t -> p c t", p=P)
                                ).then_inc(SM["a3x"], 16)

            def _a3_store(it3):
                g.wait_ge(SM["a3v"], a3v(it3, 4))
                g.dma_start(out=xp_hbm[:, :, :,
                                       it3 * TT3:(it3 + 1) * TT3],
                            in_=a3_x[:]).then_inc(SM["a3d"], 16)

            _a3_load(0)
            for k3 in range(1, NT3):
                _a3_store(k3 - 1)
                _a3_load(k3)
            _a3_store(NT3 - 1)
            g.wait_ge(SM["a3d"], 16 * NT3)

        def a3_vec(v, it):
            aq = {nm: a3_q[nm][:, it % 2] for nm in a3_q}
            v.wait_ge(SM["a3q"], 64 * (it + 1))
            v.wait_ge(SM["a3d"], 16 * it)
            for c in range(DC):
                for sq in range(NSEQ):
                    v.tensor_tensor_scan(
                        out=a3_a[:, c, sq, :], data0=aq["td"][:, c, sq, :],
                        data1=aq["ekv"][:, c, sq, :],
                        initial=a3_cr[:, c, sq, 0:1],
                        op0=OP.mult, op1=OP.add)
                    v.tensor_tensor_scan(
                        out=a3_b[:, c, sq, :], data0=aq["td"][:, c, sq, :],
                        data1=aq["ek"][:, c, sq, :],
                        initial=a3_cr[:, c, sq, 1:2],
                        op0=OP.mult, op1=OP.add)
            v.drain()
            v.tensor_copy(out=a3_cr[:, :, :, 0:1],
                          in_=a3_a[:, :, :, TT3 - 1:TT3])
            v.tensor_copy(out=a3_cr[:, :, :, 1:2],
                          in_=a3_b[:, :, :, TT3 - 1:TT3])
            v.drain().then_inc(SM["a3v"], 1)
            for c in range(DC):
                v.scalar_tensor_tensor(out=aq["ekv"][:, c, :, :],
                                       in0=aq["ekv"][:, c, :, :],
                                       scalar=gdcol[:, c:c + 1],
                                       in1=a3_a[:, c, :, :],
                                       op0=OP.mult, op1=OP.add)
                v.scalar_tensor_tensor(out=aq["ek"][:, c, :, :],
                                       in0=aq["ek"][:, c, :, :],
                                       scalar=gdcol[:, c:c + 1],
                                       in1=a3_b[:, c, :, :],
                                       op0=OP.mult, op1=OP.add)
            v.drain().then_inc(SM["a3v"], 1)
            v.tensor_scalar(out=a3_b[:].rearrange("p c s t -> p (c s t)"),
                            in0=aq["ek"].rearrange(
                                "p c s t -> p (c s t)"),
                            scalar1=0.2, scalar2=2e-07,
                            op0=OP.mult, op1=OP.add)
            v.drain()
            v.reciprocal(out=aq["ek"].rearrange("p c s t -> p (c s t)"),
                         in_=a3_b[:].rearrange("p c s t -> p (c s t)"))
            v.drain()
            v.tensor_tensor(out=aq["td"][:], in0=aq["ekv"][:],
                            in1=aq["ek"][:], op=OP.mult)
            v.drain()
            v.tensor_tensor(out=aq["r"][:], in0=aq["r"][:],
                            in1=aq["td"][:], op=OP.mult)
            v.drain().then_inc(SM["a3v"], 1)
            for tt in range(TT3):
                prev = (a3_cr[:, :, :, 2:3] if tt == 0
                        else a3_mh[:, :, :, tt - 1:tt])
                v.scalar_tensor_tensor(out=a3_cr[:, :, :, 3:4], in0=prev, scalar=1.0,
                                       in1=aq["r"][:, :, :, tt:tt + 1],
                                       op0=OP.is_gt, op1=OP.subtract)
                v.drain()
                v.scalar_tensor_tensor(out=a3_mh[:, :, :, tt:tt + 1],
                                       in0=prev, scalar=0.3, in1=a3_cr[:, :, :, 3:4],
                                       op0=OP.mult, op1=OP.subtract)
                v.drain()
            v.tensor_copy(out=a3_cr[:, :, :, 2:3],
                          in_=a3_mh[:, :, :, TT3 - 1:TT3])
            v.drain()
            v.scalar_tensor_tensor(out=a3_a[:, :, :, 0:TT3 - 1],
                                   in0=a3_mh[:, :, :, 0:TT3 - 1],
                                   scalar=0.3,
                                   in1=a3_mh[:, :, :, 1:TT3],
                                   op0=OP.mult, op1=OP.subtract)
            v.drain()
            v.tensor_tensor(out=a3_b[:, :, :, 0:TT3 - 1],
                            in0=a3_a[:, :, :, 0:TT3 - 1],
                            in1=aq["r"][:, :, :, 1:TT3], op=OP.add)
            v.tensor_scalar(out=a3_b[:, :, :, TT3 - 1:TT3],
                            in0=a3_mh[:, :, :, TT3 - 1:TT3],
                            scalar1=1.0, scalar2=None, op0=OP.is_gt)
            v.drain()
            for c in range(DC):
                for sq in range(NSEQ):
                    gab = modT[:, c, 2, sq:sq + 1].broadcast_to([P, TT3])
                    v.scalar_tensor_tensor(out=a3_a[:, c, sq, :],
                                           in0=a3_b[:, c, sq, :],
                                           scalar=0.5, in1=gab,
                                           op0=OP.is_gt, op1=OP.mult)
            v.drain()
            v.wait_ge(SM["a3x"], 64 * (it + 1))
            v.tensor_tensor(out=a3_x[:], in0=a3_a[:], in1=a3_x[:],
                            op=OP.add)
            v.drain().then_inc(SM["a3v"], 1)


        @em("vector")
        def _(v):
            v.wait_ge(SM["gp"], 2)
            for it in range(A2L):
                for sq in range(NSEQ):
                    k = it * NSEQ + sq
                    v.wait_ge(SM["ax%d" % (k % 8)], 16 * (k // 8 + 1))
                    v.tensor_reduce(out=a2_c[:, sq:sq + 1],
                                    in_=a2_xt[k % 8][:, :], axis=AX,
                                    op=OP.add)
                v.drain()
                v.tensor_scalar(out=a2_c[:, 8:12], in0=a2_c[:, 0:4],
                                scalar1=INV_D, scalar2=None, op0=OP.mult)
                v.drain()
                for sq in range(NSEQ):
                    k = it * NSEQ + sq
                    xpc = a2_xt[k % 8]
                    v.tensor_scalar(out=xpc[:, :], in0=xpc[:, :],
                                    scalar1=a2_c[:, 8 + sq:9 + sq],
                                    scalar2=None, op0=OP.subtract)
                v.drain().then_inc(SM["a2v"], 1)
                v.wait_ge(SM["a2s"], a2s_invs(it))
                for sq in range(NSEQ):
                    k = it * NSEQ + sq
                    v.tensor_scalar(out=a2_xt[k % 8][:, :],
                                    in0=a2_xt[k % 8][:, :],
                                    scalar1=a2_c[:, 16 + sq:17 + sq],
                                    scalar2=None, op0=OP.mult)
                v.drain().then_inc(SM["a2v"], 1)
                if A2_CUT == "t1":
                    continue
                v.wait_ge(SM["a2s"], a2s_hF(it))
                for c in range(DC):
                    hFc = a2_hF[:, c, :, :]
                    v.tensor_tensor(out=a2_dt[:, :, 1:TT2],
                                    in0=hFc[:, :, 1:TT2],
                                    in1=hFc[:, :, 0:TT2 - 1], op=OP.subtract)
                    v.tensor_tensor(out=a2_dt[:, :, 0:1], in0=hFc[:, :, 0:1],
                                    in1=a2_hc[:, c, :, :], op=OP.subtract)
                    v.drain()
                    v.tensor_copy(out=a2_hc[:, c, :, :],
                                  in_=hFc[:, :, TT2 - 1:TT2])
                    v.drain()
                    v.scalar_tensor_tensor(out=a2_xm[:, c, :, :],
                                           in0=a2_dt[:, :, :],
                                           scalar=c2col[:, c:c + 1],
                                           in1=hFc[:, :, :], op0=OP.mult,
                                           op1=OP.add)
                    v.drain()
                v.drain().then_inc(SM["a2v"], 1)
                if A2_CUT == "xm":
                    continue
                for idx in range(16):
                    ec, hf = divmod(idx, 2)
                    v.wait_ge(SM["a2s"], a2s_ep(it, idx))
                    nprev = (it * 16 + idx) // 2
                    if nprev >= 1 and A2_CUT != "nostore":
                        for j2 in range(2):
                            v.wait_ge(SM["adq%d" % (hf * 4 + j2)], 16 * nprev)
                    pp = (it * 16 + idx) % 3
                    v.scalar_tensor_tensor(out=a2_stg[:, hf, 0, :],
                                           in0=a2_ek[:, (it * 16 + idx) % 2,
                                                     :],
                                           scalar=E10,
                                           in1=ps_qkv[:, pp, 2, :],
                                           op0=OP.min, op1=OP.bypass)
                    v.drain()
                    v.tensor_tensor(out=a2_stg[:, hf, 1, :],
                                    in0=a2_stg[:, hf, 0, :],
                                    in1=ps_qkv[:, pp, 2, :], op=OP.mult)
                    v.drain().then_inc(SM["a2v"], 1)
                if A3_ON and it >= 1:
                    a3_vec(v, 2 * it - 2)
                    a3_vec(v, 2 * it - 1)
                if A3_ON and it == A2L - 1 and A2L == NT2:
                    a3_vec(v, 2 * it)
                    a3_vec(v, 2 * it + 1)

        @em("scalar")
        def _(s):
            for it in range(A2L):
                s.wait_ge(SM["a2v"], a2v_z(it))
                for sq in range(NSEQ):
                    k = it * NSEQ + sq
                    s.activation(out=a2_hF[:, 0:2, :, :].rearrange(
                                     "p c s t -> p (c s t)"),
                                 in_=a2_xt[k % 8][:, :],
                                 func=AF.Square,
                                 accum_out=a2_c[:, 4 + sq:5 + sq])
                    s.drain()
                s.activation(out=a2_c[:, 12:16], in_=a2_c[:, 4:8],
                             func=AF.Ln, scale=INV_D, bias=c_lneps[:])
                s.drain()
                s.activation(out=a2_c[:, 16:20], in_=a2_c[:, 12:16],
                             func=AF.Exp, scale=-0.5)
                s.drain().then_inc(SM["a2s"], 1)
                if A2_CUT == "t1":
                    continue
                for j in range(32):
                    sq, dsub = divmod(j, DC)
                    s.wait_ge(SM["a2t"], a2t_tr(it, j))
                    s.activation(out=a2_hF[:, dsub, sq, :],
                                 in_=ps_trf[:, j % 2, 0:TT2],
                                 func=AF.Identity,
                                 scale=modT[:, dsub, 1, sq:sq + 1],
                                 bias=modT[:, dsub, 0, sq:sq + 1])
                    s.drain().then_inc(SM["a2s"], 1)
                if A2_CUT == "xm":
                    continue
                for idx in range(16):
                    ec, hf = divmod(idx, 2)
                    s.wait_ge(SM["a2t"], a2t_mm(it, idx))
                    if it * 16 + idx >= 2:
                        pi, px = divmod(it * 16 + idx - 2, 16)
                        s.wait_ge(SM["a2v"], a2v_ep(pi, px))
                    nprev = (it * 16 + idx) // 2
                    if nprev >= 1 and A2_CUT != "nostore":
                        for j2 in range(2, 4):
                            s.wait_ge(SM["adq%d" % (hf * 4 + j2)], 16 * nprev)
                    pp = (it * 16 + idx) % 3
                    s.activation(out=a2_ek[:, (it * 16 + idx) % 2, :],
                                 in_=ps_qkv[:, pp, 1, :],
                                 func=AF.Exp)
                    s.activation(out=a2_stg[:, hf, 2, :],
                                 in_=ps_qkv[:, pp, 3, :], func=AF.Exp)
                    s.activation(out=a2_stg[:, hf, 3, :],
                                 in_=ps_qkv[:, pp, 0, :], func=AF.Sigmoid)
                    s.drain().then_inc(SM["a2s"], 1)

        @em("tensor")
        def _(t):
            for it in range(A2L):
                if A2_CUT == "t1":
                    continue
                for j in range(32):
                    sq, dsub = divmod(j, DC)
                    if j % DC == 0:
                        t.wait_ge(SM["a2v"], a2v_t1(it))
                    if it * 32 + j >= 2:
                        pi, pj = divmod(it * 32 + j - 2, 32)
                        t.wait_ge(SM["a2s"], pi * A2S + 1 + pj + 1)
                    t.transpose(ps_trf[:, j % 2, 0:P],
                                a2_xt[(it * NSEQ + sq) % 8][:,
                                      dsub * P:(dsub + 1) * P],
                                identf[:]).then_inc(SM["a2t"], 1)
                if A2_CUT == "xm":
                    continue
                t.wait_ge(SM["a2v"], a2v_xm(it))
                for idx in range(16):
                    ec, hf = divmod(idx, 2)
                    st2 = it * DC + ec
                    t.wait_ge(SM["aw%d" % (st2 % 2)], 64 * (st2 // 2 + 1))
                    if it * 16 + idx >= 3:
                        pi, px = divmod(it * 16 + idx - 3, 16)
                        t.wait_ge(SM["a2v"], pi * A2V + 3 + px + 1)
                        t.wait_ge(SM["a2s"], pi * A2S + 33 + px + 1)
                    pp = (it * 16 + idx) % 3
                    xmf = a2_xm[:, :, :, :].rearrange("p c s t -> p c (s t)")
                    mi = None
                    for qi in range(4):
                        wb = a2_w[(st2 % 2) * 4 + qi]
                        for dsub in range(DC):
                            mi = t.matmul(ps_qkv[:, pp, qi, :], wb[:, dsub, :],
                                          xmf[:, dsub, hf * HC2:(hf + 1) * HC2],
                                          start=(dsub == 0),
                                          stop=(dsub == DC - 1))
                    mi.then_inc(SM["a2t"], 1)

        A3V = 4

        def a3v(it, stage):
            return it * A3V + stage

        a_st.close()
        trf_st.close()
        barrier()
        PHB[2] = {e: len(PL[e]) for e in PL}

        b_st = ExitStack()
        bsb = lambda n_, s, d=F32: b_st.enter_context(nc.sbuf_tensor(n_, s, d))
        b_in = bsb("b_in", [P, I])
        b_U = bsb("b_U", [P, I])
        b_c1 = bsb("b_c1", [P, I])
        b_c2t = bsb("b_c2t", [P, I])
        b_M = bsb("b_M", [P, I // 16])
        b_R = bsb("b_R", [P, I // 16])
        b_sc = bsb("b_sc", [P, I // 16])
        b_nsc = bsb("b_nsc", [P, I // 16])
        b_xq = bsb("b_xq", [P, I], BF16)
        b_trs = bsb("b_trs", [P, 2, P], BF16)
        b_xp = bsb("b_xp", [P, 2, DC, TTB])
        bc2 = bsb("bc2", [P, D])
        b_z = bsb("b_z", [P, D])
        b_h2 = bsb("b_h2", [P, D])
        b_qF = bsb("b_qF", [P, NSEQ, DC, P], BF16)
        b_xq4 = bsb("b_xq4", [P, NSEQ, I], BF16)
        b_hmsl = bsb("b_hmsl", [P, 2, 512], F32)
        b_hqF = bsb("b_hqF", [P, IC, TTB], BF16)
        b_wgu = [bsb("b_wgu%d" % i, [P, DC, 1024], BF16) for i in range(2)]
        b_wd = [bsb("b_wd%d" % i, [P, D], BF16) for i in range(2)]
        b_sg = bsb("b_sg", [P, 512], F32)
        b_stc = bsb("b_stc", [P, 8], F32)
        ps_a = b_st.enter_context(nc.psum_tensor("ps_a", [P, D], F32))
        ps_gu = b_st.enter_context(nc.psum_tensor("ps_gu", [P, 2, 1024], F32))
        ps_tb2 = b_st.enter_context(nc.psum_tensor("ps_tb2", [P, 2, D], BF16))

        def quant_emit(v, in_x, xq, F):
            return _quant_emit(v, in_x, xq, F)

        def quant_emit2(v, in_x, xq_ap, F):
            return _quant_emit(v, in_x, xq_ap, F, ap_out=True)

        def _quant_emit(v, in_x, xq, F, ap_out=False, split=None):
            nb = F // 16
            x3 = in_x.rearrange("p (b k) -> p b k", k=16)
            v.tensor_reduce(out=b_M[:, 0:nb].rearrange(
                "p (b o) -> p b o", o=1), in_=x3, axis=AX, op=OP.max,
                apply_absolute_value=True)
            v.drain()
            v.tensor_scalar(out=b_sc[:, 0:nb], in0=b_M[:, 0:nb],
                            scalar1=float(np.float32(1.0) / np.float32(6.0)),
                            scalar2=float(np.float32(1e-12)),
                            op0=OP.mult, op1=OP.max)
            v.drain()
            v.reciprocal(out=b_R[:, 0:nb], in_=b_sc[:, 0:nb])
            v.tensor_scalar(out=b_nsc[:, 0:nb], in0=b_sc[:, 0:nb],
                            scalar1=-1.0, scalar2=None, op0=OP.mult)
            v.drain()
            Rb = b_R[:, 0:nb].rearrange("p (b o) -> p b o", o=1).broadcast_to(
                [P, nb, 16])
            scb = b_sc[:, 0:nb].rearrange("p (b o) -> p b o",
                                          o=1).broadcast_to([P, nb, 16])
            nscb = b_nsc[:, 0:nb].rearrange("p (b o) -> p b o",
                                            o=1).broadcast_to([P, nb, 16])
            U3 = b_U[:, 0:F].rearrange("p (b k) -> p b k", k=16)
            v.tensor_tensor(out=U3, in0=x3, in1=Rb, op=OP.mult)
            v.drain()
            ss_ = b_in[:, 0:F].rearrange("p (b k) -> p b k", k=16)
            v.scalar_tensor_tensor(out=ss_, in0=U3, scalar=1e30, in1=scb,
                                   op0=OP.mult, op1=OP.min)
            v.drain()
            v.scalar_tensor_tensor(out=b_U[:, 0:F], in0=b_U[:, 0:F],
                                   scalar=-1.0, in1=b_U[:, 0:F], op0=OP.mult,
                                   op1=OP.max)
            v.scalar_tensor_tensor(out=ss_, in0=ss_, scalar=1.0, in1=nscb,
                                   op0=OP.mult, op1=OP.max)
            v.drain()
            zb = zcol[:].broadcast_to([P, F])
            v.scalar_tensor_tensor(out=b_c1[:, 0:F], in0=b_U[:, 0:F],
                                   scalar=0.25, in1=zb, op0=OP.is_gt,
                                   op1=OP.add)
            v.scalar_tensor_tensor(out=b_c2t[:, 0:F], in0=b_U[:, 0:F],
                                   scalar=5.0, in1=zb, op0=OP.is_gt,
                                   op1=OP.add)
            v.drain()
            v.scalar_tensor_tensor(out=b_c1[:, 0:F], in0=b_U[:, 0:F],
                                   scalar=0.75, in1=b_c1[:, 0:F],
                                   op0=OP.is_gt, op1=OP.add)
            v.scalar_tensor_tensor(out=b_c2t[:, 0:F], in0=b_U[:, 0:F],
                                   scalar=5.0, in1=b_c2t[:, 0:F],
                                   op0=OP.is_gt, op1=OP.add)
            v.drain()
            v.scalar_tensor_tensor(out=b_c1[:, 0:F], in0=b_U[:, 0:F],
                                   scalar=1.25, in1=b_c1[:, 0:F],
                                   op0=OP.is_gt, op1=OP.add)
            v.scalar_tensor_tensor(out=b_c2t[:, 0:F], in0=b_U[:, 0:F],
                                   scalar=3.5, in1=b_c2t[:, 0:F],
                                   op0=OP.is_gt, op1=OP.add)
            v.drain()
            v.scalar_tensor_tensor(out=b_c1[:, 0:F], in0=b_U[:, 0:F],
                                   scalar=1.75, in1=b_c1[:, 0:F],
                                   op0=OP.is_gt, op1=OP.add)
            v.scalar_tensor_tensor(out=b_c2t[:, 0:F], in0=b_U[:, 0:F],
                                   scalar=2.5, in1=b_c2t[:, 0:F],
                                   op0=OP.is_gt, op1=OP.add)
            v.drain()
            v.scalar_tensor_tensor(out=b_c1[:, 0:F], in0=b_c1[:, 0:F],
                                   scalar=0.5, in1=b_c2t[:, 0:F],
                                   op0=OP.mult, op1=OP.add)
            v.drain()
            if split:
                v.tensor_tensor(out=xq, in0=b_c1[:, 0:F].rearrange(
                    "p (a b) -> p a b", b=split),
                    in1=b_in[:, 0:F].rearrange("p (a b) -> p a b", b=split),
                    op=OP.mult)
            else:
                xq_out = xq if ap_out else xq[:, 0:F]
                v.tensor_tensor(out=xq_out, in0=b_c1[:, 0:F],
                                in1=b_in[:, 0:F], op=OP.mult)
            v.drain()

        wq_tiles = ([("g", t_) for t_ in range(IC)]
                    + [("u", t_) for t_ in range(IC)]
                    + [("d", t_) for t_ in range(DC)])
        n_pieces = [(D if kd in ("g", "u") else I) // P for kd, _ in wq_tiles]
        cum_pieces = np.cumsum([0] + n_pieces).tolist()

        @em("sync")
        def _(s):
            for k, (kind, t_) in enumerate(wq_tiles):
                if k >= 1:
                    s.wait_ge(SM["wqv"], k)
                src = {"g": wg, "u": wu, "d": wd}[kind]
                F = D if kind in ("g", "u") else I
                s.dma_start(out=b_in[:, 0:F],
                            in_=src[t_ * P:(t_ + 1) * P, :]).then_inc(
                    SM["wqd"], 16)

        @em("vector")
        def _(v):
            for k, (kind, t_) in enumerate(wq_tiles):
                v.wait_ge(SM["wqd"], 16 * (k + 1))
                if k >= 1:
                    v.wait_ge(SM["wqt"], cum_pieces[k])
                F = D if kind in ("g", "u") else I
                quant_emit(v, b_in[:, 0:F], b_xq, F)
                v.drain().then_inc(SM["wqv"], 1)

        @em("tensor")
        def _(t):
            cnt = 0
            for k, (kind, t_) in enumerate(wq_tiles):
                t.wait_ge(SM["wqv"], k + 1)
                for j in range(n_pieces[k]):
                    cnt += 1
                    if cnt > 2:
                        t.wait_ge(SM["wqs"], cnt - 2)
                    t.transpose(ps_tb2[:, cnt % 2, 0:P],
                                b_xq[:, j * P:(j + 1) * P],
                                identb[:]).then_inc(SM["wqt"], 1)

        @em("scalar")
        def _(s):
            cnt = 0
            for k, (kind, t_) in enumerate(wq_tiles):
                for j in range(n_pieces[k]):
                    cnt += 1
                    s.wait_ge(SM["wqt"], cnt)
                    if cnt > 2:
                        s.wait_ge(SM["wo%d" % (cnt % 2)], 16 * ((cnt - 2 + 1) // 2))
                    s.copy(out=b_trs[:, cnt % 2, :],
                           in_=ps_tb2[:, cnt % 2, 0:P])
                    s.drain().then_inc(SM["wqs"], 1)
                    if kind == "d":
                        dst = wd_hbm[:, j, t_ * P:(t_ + 1) * P]
                    else:
                        if t_ < 20:
                            sl2, off = divmod(t_, 4)
                            col = (sl2 * 1024 + (512 if kind == "u" else 0)
                                   + off * P)
                        else:
                            col = (5120 + (256 if kind == "u" else 0)
                                   + (t_ - 20) * P)
                        dst = wgu_hbm[:, j, col:col + P]
                    s.dma_start(out=dst, in_=b_trs[:, cnt % 2, :]).then_inc(
                        SM["wo%d" % (cnt % 2)], 16)

        @em("sync")
        def _(s):
            for k2 in range(2):
                s.wait_ge(SM["wo%d" % k2], 16 * ((cum_pieces[-1] + 1 - k2) // 2))

        PHB[3] = {e: len(PL[e]) for e in PL}
        SLC = [(0, 1024, 0, 512), (1024, 1024, 0, 512), (2048, 1024, 0, 512),
               (3072, 1024, 0, 512), (4096, 1024, 0, 512),
               (5120, 512, 0, 256)]

        def _bc(it):
            return 1 if it % TBS == 0 else 0

        def bv_base(it):
            return sum(2 * _bc(jj) + 40 for jj in range(it))

        def bv_z(it, j):
            return bv_base(it) + 2 * _bc(it) + 2 * j + 1

        def bv_q(it, j):
            return bv_base(it) + 2 * _bc(it) + 2 * j + 2

        def bv_hm(it, j, sl):
            return bv_base(it) + 2 * _bc(it) + 8 + sl * 4 + j + 1

        def bv_dn(it, n):
            return bv_base(it) + 2 * _bc(it) + 32 + n

        def bv_end(it):
            return bv_base(it + 1)

        def bs_base(it):
            return it * 44

        def bs_invs(it, j):
            return bs_base(it) + 2 * j + 1

        def bs_qF(it, j):
            return bs_base(it) + 2 * j + 2

        def bs_silu(it, j, sl):
            return bs_base(it) + 8 + sl * 4 + j + 1

        def bs_hqF(it, j, grp):
            return bs_base(it) + 32 + j * 3 + grp + 1

        def bt_base(it):
            return sum(2 * _bc(jj) + 46 for jj in range(it))

        def bt_xT(it, j):
            return bt_base(it) + 2 * _bc(it) + 2 * j + 1

        def bt_xq(it, j):
            return bt_base(it) + 2 * _bc(it) + 2 * j + 2

        def bt_gu(it, j, sl):
            return bt_base(it) + 2 * _bc(it) + 8 + sl * 4 + j + 1

        def bt_hmtr(it, j, grp):
            return bt_base(it) + 2 * _bc(it) + 32 + j * 3 + grp + 1

        def bt_dn(it, ch):
            return bt_base(it) + 2 * _bc(it) + 44 + ch + 1

        HMG = [(0, 8), (8, 16), (16, 22)]

        @em("sync")
        def _(s):
            for it in range(NTB):
                seq, tb = divmod(it, TBS)
                if it >= 1:
                    s.wait_ge(SM["bo"], 16 * it)
                s.dma_start(out=b_xp[:],
                            in_=xp_hbm[:, :, seq, tb * TTB:(tb + 1) * TTB]
                            ).then_inc(SM["bx"], 16)
                for sl in range(6):
                    g = it * 6 + sl
                    if g >= 2:
                        pi, pl = divmod(g - 2, 6)
                        s.wait_ge(SM["bt"], bt_gu(pi, 3, pl))
                    col, w = SLC[sl][0], SLC[sl][1]
                    s.dma_start(out=b_wgu[g % 2][:, :, 0:w],
                                in_=wgu_hbm[:, :, col:col + w]
                                ).then_inc(SM["bw%d" % (g % 2)], 16)
                for ch in range(2):
                    for isub in range(IC):
                        g = (it * 2 + ch) * IC + isub
                        if g >= 2:
                            s.wait_ge(SM["bwd"], 16 * (g - 1) + 1)
                        s.dma_start(out=b_wd[g % 2][:, :],
                                    in_=wd_hbm[:, isub, :]).then_inc(
                            SM["bd%d" % (g % 2)], 16)
                s.wait_ge(SM["bv"], bv_end(it))
                s.dma_start(out=out[:, :, seq, tb * TTB:(tb + 1) * TTB],
                            in_=b_xp[:]).then_inc(SM["bo"], 16)
            s.wait_ge(SM["bo"], 16 * NTB)
        PL["sync"].pop()


        @em("sync")
        def _(s):
            for it in range(NTB):
                seq, tb = divmod(it, TBS)
                if it >= 2:
                    s.wait_ge(SM["bo"], 16 * (it - 1))
                s.dma_start(out=b_xp[:, it % 2],
                            in_=xp_hbm[:, :, seq, tb * TTB:(tb + 1) * TTB]
                            ).then_inc(SM["bx"], 16)
                for sl in range(6):
                    g = it * 6 + sl
                    if g >= 2:
                        pi, pl = divmod(g - 2, 6)
                        s.wait_ge(SM["bt"], bt_gu(pi, 3, pl))
                    col, w = SLC[sl][0], SLC[sl][1]
                    s.dma_start(out=b_wgu[g % 2][:, :, 0:w],
                                in_=wgu_hbm[:, :, col:col + w]
                                ).then_inc(SM["bw%d" % (g % 2)], 16)
                for hp in range(2):
                    for isub in range(IC):
                        g = (it * 2 + hp) * IC + isub
                        if g >= 2:
                            s.wait_ge(SM["a3q"], NT3 * 64 + g - 1)
                        s.dma_start(out=b_wd[g % 2][:, 0:512],
                                    in_=wd_hbm[:, isub,
                                               hp * 512:(hp + 1) * 512]
                                    ).then_inc(SM["bd%d" % (g % 2)], 16)
                s.wait_ge(SM["bv"], bv_end(it))
                s.dma_start(out=out[:, :, seq, tb * TTB:(tb + 1) * TTB],
                            in_=b_xp[:, it % 2]).then_inc(SM["bo"], 16)
            s.wait_ge(SM["bo"], 16 * NTB)

        @em("tensor")
        def _(t):
            for it in range(NTB):
                seq, tb = divmod(it, TBS)
                if it % TBS == 0:
                    t.wait_ge(SM["bv"], bv_base(it))
                    t.wait_ge(SM["gp"], 2)
                    mi = None
                    for half in range(2):
                        mi = t.matmul(ps_a[:, half * 512:(half + 1) * 512],
                                      ssel[:, seq, :],
                                      modkeep[0:NSEQ, D + half * 512:
                                              D + (half + 1) * 512],
                                      start=True, stop=True)
                    mi.then_inc(SM["bt"], 1)
                    t.wait_ge(SM["bv"], bv_base(it) + 1)
                    mi = None
                    for half in range(2):
                        mi = t.matmul(ps_a[:, half * 512:(half + 1) * 512],
                                      ssel[:, seq, :],
                                      modkeep[0:NSEQ,
                                              half * 512:(half + 1) * 512],
                                      start=True, stop=True)
                    mi.then_inc(SM["bt"], 1)
                for j in range(4):
                    t.wait_ge(SM["bx"], 16 * (it + 1))
                    if it % TBS == 0:
                        t.wait_ge(SM["bv"], bv_base(it) + 2)
                    if j >= 1:
                        t.wait_ge(SM["bv"], bv_z(it, j - 1))
                    elif it >= 1:
                        t.wait_ge(SM["bv"], bv_z(it - 1, 3))
                    mi = None
                    for dsub in range(DC):
                        mi = t.transpose(ps_a[:, dsub * P:(dsub + 1) * P],
                                         b_xp[:, it % 2, dsub,
                                              j * P:(j + 1) * P],
                                         identf[:])
                    mi.then_inc(SM["bt"], 1)
                    t.wait_ge(SM["bv"], bv_q(it, j))
                    if j >= 1:
                        t.wait_ge(SM["bs"], bs_qF(it, j - 1))
                    elif it >= 1:
                        t.wait_ge(SM["bs"], bs_hqF(it - 1, 3, 2))
                    mi = None
                    for dsub in range(DC):
                        mi = t.transpose(
                            ps_tb2[:, j % 2, dsub * P:(dsub + 1) * P],
                            b_xq[:, dsub * P:(dsub + 1) * P], identb[:])
                    mi.then_inc(SM["bt"], 1)
                if it >= 1:
                    t.wait_ge(SM["bv"], bv_base(it))
                for sl in range(6):
                    g = it * 6 + sl
                    col, w, go, uo = SLC[sl]
                    t.wait_ge(SM["bw%d" % (g % 2)], 16 * (g // 2 + 1))
                    for j in range(4):
                        t.wait_ge(SM["bs"], bs_qF(it, j))
                        gg = sl * 4 + j
                        if it * 24 + gg >= 2:
                            tot = it * 24 + gg - 2
                            pit, rem = divmod(tot, 24)
                            psl, pj = divmod(rem, 4)
                            t.wait_ge(SM["bv"], bv_hm(pit, pj, psl))
                            t.wait_ge(SM["bs"], bs_silu(pit, pj, psl))
                        pp = gg % 2
                        mi = None
                        for dsub in range(DC):
                            for hw_ in range(0, w, 512):
                                mi = t.matmul(
                                    ps_gu[:, pp, hw_:hw_ + 512],
                                    b_qF[:, j, dsub, :],
                                    b_wgu[g % 2][:, dsub, hw_:hw_ + 512],
                                    start=(dsub == 0), stop=(dsub == DC - 1))
                        mi.then_inc(SM["bt"], 1)
                for j in range(4):
                    t.wait_ge(SM["bv"], bv_hm(it, j | 1, 5))
                    for grp, (i0_, i1_) in enumerate(HMG):
                        if j == 0 and grp == 0:
                            t.wait_ge(SM["bs"], bs_qF(it, 3))
                        elif grp == 0:
                            t.wait_ge(SM["bs"], bs_hqF(it, j - 1, 2))
                        else:
                            t.wait_ge(SM["bs"], bs_hqF(it, j, grp - 1))
                        pg = (j * 3 + grp) % 2
                        mi = None
                        for isub in range(i0_, i1_):
                            mi = t.transpose(
                                ps_tb2[:, pg, (isub - i0_) * P:
                                       (isub - i0_ + 1) * P],
                                b_xq4[:, j, isub * P:(isub + 1) * P],
                                identb[:])
                        mi.then_inc(SM["bt"], 1)
                t.wait_ge(SM["bs"], bs_hqF(it, 3, 2))
                t.wait_ge(SM["bv"], bv_hm(it, 3, 5))
                psd = ps_gu[:, :, :].rearrange("p a b -> p (a b)").rearrange(
                    "p (h c) -> p h c", h=4)
                for hp in range(2):
                    if hp == 1:
                        t.wait_ge(SM["bv"], bv_dn(it, 4))
                    for isub in range(IC):
                        g = (it * 2 + hp) * IC + isub
                        t.wait_ge(SM["bd%d" % (g % 2)], 16 * (g // 2 + 1))
                        mi = None
                        for h4 in range(4):
                            mi = t.matmul(psd[:, h4, :],
                                          b_wd[g % 2][:, h4 * P:(h4 + 1) * P],
                                          b_hqF[:, isub, :],
                                          start=(isub == 0),
                                          stop=(isub == IC - 1))
                        mi.then_inc(SM["a3q"], 1)
                    t.drain().then_inc(SM["bt"], 1)

        @em("scalar")
        def _(s):
            for it in range(NTB):
                for j in range(4):
                    s.wait_ge(SM["bv"], bv_z(it, j))
                    s.activation(out=b_h2[:, :], in_=b_z[:, :],
                                 func=AF.Square, accum_out=b_stc[:, 0:1])
                    s.drain()
                    s.activation(out=b_stc[:, 2:3], in_=b_stc[:, 0:1],
                                 func=AF.Ln, scale=INV_D, bias=c_lneps[:])
                    s.drain()
                    s.activation(out=b_stc[:, 3:4], in_=b_stc[:, 2:3],
                                 func=AF.Exp, scale=-0.5)
                    s.drain().then_inc(SM["bs"], 1)
                    s.wait_ge(SM["bt"], bt_xq(it, j))
                    if it >= 1:
                        s.wait_ge(SM["bt"], bt_gu(it - 1, j, 5))
                    s.copy(out=b_qF[:, j, :, :].rearrange(
                        "p c e -> p (c e)"), in_=ps_tb2[:, j % 2, :])
                    s.drain().then_inc(SM["bs"], 1)
                for sl in range(6):
                    col, w, go, uo = SLC[sl]
                    for j in range(4):
                        s.wait_ge(SM["bt"], bt_gu(it, j, sl))
                        gg = it * 24 + sl * 4 + j
                        if gg >= 1:
                            pit, rem = divmod(gg - 1, 24)
                            psl, pj = divmod(rem, 4)
                            s.wait_ge(SM["bv"], bv_hm(pit, pj, psl))
                        pp = (sl * 4 + j) % 2
                        s.activation(out=b_sg[:, 0:uo],
                                     in_=ps_gu[:, pp, go:go + uo],
                                     func=AF.Silu)
                        s.drain().then_inc(SM["bs"], 1)
                for j in range(4):
                    for grp, (i0_, i1_) in enumerate(HMG):
                        s.wait_ge(SM["bt"], bt_hmtr(it, j, grp))
                        pg = (j * 3 + grp) % 2
                        for isub in range(i0_, i1_):
                            s.copy(out=b_hqF[:, isub, j * P:(j + 1) * P],
                                   in_=ps_tb2[:, pg, (isub - i0_) * P:
                                              (isub - i0_ + 1) * P])
                        s.drain().then_inc(SM["bs"], 1)

        @em("vector")
        def _(v):
            for it in range(NTB):
                seq, tb = divmod(it, TBS)
                if it % TBS == 0:
                    v.wait_ge(SM["bt"], bt_base(it) + 1)
                    v.tensor_copy(out=bc1[:, :], in_=ps_a[:, :])
                    v.drain().then_inc(SM["bv"], 1)
                    v.wait_ge(SM["bt"], bt_base(it) + 2)
                    v.tensor_copy(out=bc2[:, :], in_=ps_a[:, :])
                    v.drain().then_inc(SM["bv"], 1)
                for j in range(4):
                    v.wait_ge(SM["bt"], bt_xT(it, j))
                    v.tensor_copy(out=b_z[:, :], in_=ps_a[:, :])
                    v.drain()
                    v.tensor_reduce(out=b_stc[:, 1:2], in_=b_z[:, :], axis=AX,
                                    op=OP.add)
                    v.drain()
                    v.tensor_scalar(out=b_stc[:, 1:2], in0=b_stc[:, 1:2],
                                    scalar1=INV_D, scalar2=None, op0=OP.mult)
                    v.drain()
                    v.tensor_scalar(out=b_z[:, :], in0=b_z[:, :],
                                    scalar1=b_stc[:, 1:2], scalar2=None,
                                    op0=OP.subtract)
                    v.drain().then_inc(SM["bv"], 1)
                    v.wait_ge(SM["bs"], bs_invs(it, j))
                    if j >= 1:
                        v.wait_ge(SM["bt"], bt_xq(it, j - 1))
                    elif it >= 1:
                        v.wait_ge(SM["bt"], bt_xq(it - 1, 3))
                    v.tensor_scalar(out=b_h2[:, :], in0=b_z[:, :],
                                    scalar1=b_stc[:, 3:4], scalar2=None,
                                    op0=OP.mult)
                    v.drain()
                    v.tensor_tensor(out=b_h2[:, :], in0=b_h2[:, :],
                                    in1=bc1[:, :], op=OP.mult)
                    v.drain()
                    v.tensor_tensor(out=b_h2[:, :], in0=b_h2[:, :],
                                    in1=bc2[:, :], op=OP.add)
                    v.drain()
                    quant_emit(v, b_h2[:, :], b_xq, D)
                    v.drain().then_inc(SM["bv"], 1)
                for sl in range(6):
                    col, w, go, uo = SLC[sl]
                    for j in range(4):
                        v.wait_ge(SM["bs"], bs_silu(it, j, sl))
                        pp = (sl * 4 + j) % 2
                        if uo == 512:
                            dst = b_hmsl[:, j % 2, 0:uo]
                        else:
                            dst = b_hmsl[:, 0, (j % 2) * uo:(j % 2 + 1) * uo]
                        v.tensor_tensor(out=dst, in0=b_sg[:, 0:uo],
                                        in1=ps_gu[:, pp, uo:uo + uo],
                                        op=OP.mult)
                        v.drain()
                        if j % 2 == 1:
                            hmf = b_hmsl[:].rearrange("p a b -> p (a b)")
                            _quant_emit(v, hmf[:, 0:2 * uo],
                                        b_xq4[:, j - 1:j + 1,
                                              sl * 512:sl * 512 + uo],
                                        2 * uo, ap_out=True, split=uo)
                        v.drain().then_inc(SM["bv"], 1)
                psdv = ps_gu[:, :, :].rearrange("p a b -> p (a b)").rearrange(
                    "p (h c) -> p h c", h=4)
                for hp in range(2):
                    v.wait_ge(SM["bt"], bt_dn(it, hp))
                    for h4 in range(4):
                        hsub = hp * 4 + h4
                        v.scalar_tensor_tensor(
                            out=b_xp[:, it % 2, hsub, :],
                            in0=psdv[:, h4, :],
                            scalar=modT[:, hsub, 5, seq:seq + 1],
                            in1=b_xp[:, it % 2, hsub, :],
                            op0=OP.mult, op1=OP.add)
                        v.drain().then_inc(SM["bv"], 1)

        if phase_limit < 99:
            cut = PHB.get(phase_limit)
            if cut:
                for e in PL:
                    PL[e] = PL[e][:cut[e]]
        block = st_.enter_context(nc.Block())

        @block.sync
        def _(e):
            for fn in PL["sync"]:
                fn(e)

        @block.vector
        def _(e):
            for fn in PL["vector"]:
                fn(e)

        @block.scalar
        def _(e):
            for fn in PL["scalar"]:
                fn(e)

        @block.tensor
        def _(e):
            for fn in PL["tensor"]:
                fn(e)

        @block.gpsimd
        def _(e):
            for fn in PL["gpsimd"]:
                fn(e)

        b_st.close()

    return nc


def kernel(**inputs):
    x = np.ascontiguousarray(inputs["L0_attn_embeddings"], dtype=np.float32)
    nc = build()
    in_maps = []
    for core in range(8):
        sl = slice(core * NSEQ, (core + 1) * NSEQ)
        xs = x[sl]
        m = {
            "xTok": np.ascontiguousarray(xs),
            "xT": np.ascontiguousarray(xs.transpose(0, 2, 1)),
            "wR": np.ascontiguousarray(inputs["L0_attn_W_R"].T),
            "wK": np.ascontiguousarray(inputs["L0_attn_W_K"].T),
            "wV": np.ascontiguousarray(inputs["L0_attn_W_V"].T),
            "wTd": np.ascontiguousarray(inputs["L0_attn_W_td"].T),
            "gdec": np.ascontiguousarray(
                np.asarray(inputs["L0_attn_global_decay"],
                           dtype=np.float32).reshape(DC, P).T),
            "mixc": np.ascontiguousarray(
                np.asarray(inputs["L0_attn_mix"],
                           dtype=np.float32).reshape(DC, P).T),
            "wg": np.ascontiguousarray(inputs["L0_ffn_w_gate"]),
            "wu": np.ascontiguousarray(inputs["L0_ffn_w_up"]),
            "wd": np.ascontiguousarray(inputs["L0_ffn_w_down"]),
            "cc": np.ascontiguousarray(inputs["L0_c"][sl]),
            "wmod": np.ascontiguousarray(inputs["L0_wmod"]),
        }
        in_maps.append(m)
    res = run_bass_kernel_spmd(nc, in_maps, core_ids=list(range(8)))
    outs = []
    for core in range(8):
        o = res.results[core]["out"]
        y = o.transpose(2, 3, 1, 0).reshape(NSEQ, T, D)
        outs.append(y)
    return np.concatenate(outs, axis=0)
